# version 08 (final)
# baseline (speedup 1.0000x reference)
import contextlib
import numpy as np
import concourse.bass as bass
import concourse.mybir as mybir
from concourse.bass_utils import run_bass_kernel_spmd

F32 = mybir.dt.float32
BF16 = mybir.dt.bfloat16
AX = mybir.AxisListType.X
ALU = mybir.AluOpType
ACT = mybir.ActivationFunctionType

NCORE = 8
DEBUG_DUMPS = False
T = 1024
D = 2048
CH = 16
ISQ = float(1.0 / np.sqrt(128.0))
LN_EPS = 1e-5
RMS_EPS = 1e-6
NE = 8


class Ctx:
    def __init__(self, nc, es):
        self.h = {}
        self.c = {}
        for n in ("qS", "qG", "qU", "qV", "CC", "T", "V", "S", "G"):
            self.h[n] = es.enter_context(nc.semaphore(n))
            self.c[n] = 0
        self.waited = {e: {} for e in ("tensor", "vector", "scalar", "gpsimd", "sync")}

    def inc(self, inst, sem, n=1):
        self.c[sem] += n
        inst.then_inc(self.h[sem], n)
        return (sem, self.c[sem])

    def now(self, sem):
        return (sem, self.c[sem])

    def wait(self, eng, ename, *toks):
        for tok in toks:
            if tok is None:
                continue
            sem, val = tok
            if val <= 0:
                continue
            if self.waited[ename].get(sem, 0) >= val:
                continue
            eng.wait_ge(self.h[sem], val)
            self.waited[ename][sem] = val


def build_nc(trunc=99):
    nc = bass.Bass("TRN2", target_bir_lowering=False, debug=False)

    def din(name, shape, dtype=F32):
        return nc.dram_tensor(name, list(shape), dtype, kind="ExternalInput").ap()

    def dout(name, shape, dtype=F32):
        return nc.dram_tensor(name, list(shape), dtype, kind="ExternalOutput").ap()

    def dint(name, shape, dtype=F32):
        return nc.dram_tensor(name, list(shape), dtype, kind="Internal").ap()

    xT = din("xT", [CH, 128, T])
    wqkv0 = din("wqkv0", [18, 128, 16, 128])
    bq = din("bq", [128, 16])
    bk = din("bk", [128, 1])
    bv = din("bv", [1, 128])
    wfc = din("wfc", [16, 128, 16, 128])
    bfc = din("bfc", [128, 16])
    a_t = din("a_t", [4, 128, 16, 128])
    bw_f = din("bw_f", [16, 128, 4, 128])
    wg_t = din("wg_t", [1, 128, 16, 8])
    c01 = din("c01", [2, 2, 128])
    wmod = din("wmod", [2, 2, 128, 12288])
    slope = din("slope", [1, 16])
    wqkv1 = din("wqkv1", [48, 128, 16, 128])
    wgate1 = din("wgate1", [16, 128, 16, 128])
    wout1 = din("wout1", [16, 128, 16, 128])
    nw1 = din("nw1", [128, 16])
    wrt = din("wrt", [1, 128, 16, 8])
    w1m = din("w1m", [NE, 8, 128, 16, 128])
    w2m = din("w2m", [NE, 8, 128, 16, 128])
    w3m = din("w3m", [NE, 16, 128, 8, 128])
    evec = din("evec", [1, 32])
    coff = din("coff", [1, 128])

    outT = dout("outT", [CH, 128, T])
    if DEBUG_DUMPS:
        xd1 = dout("xd1", [CH, 128, T])
        xd3 = dout("xd3", [CH, 128, T])

    mod_in = dint("mod_in", [2, 12288])
    mod_out = dint("mod_out", [2, 12288])
    xsp = dint("xsp", [CH, 128, T])
    ag0_in = dint("ag0_in", [256, T], BF16)
    ag0_out = dint("ag0_out", [NCORE * 256, T], BF16)
    agu_in = [dint(f"agu_in{g}", [128, 1024], F32) for g in range(4)]
    agu_out = [dint(f"agu_out{g}", [NCORE * 128, 1024], F32) for g in range(4)]
    flat_w = dint("flat_w", [16, 32])
    flat_a = dint("flat_a", [16, 1])
    flat_b = dint("flat_b", [16, 1])

    es = contextlib.ExitStack()
    with es:
        sb = lambda n, s, d: es.enter_context(nc.sbuf_tensor(n, list(s), d))
        sbx = lambda st, n, s, d: st.enter_context(nc.sbuf_tensor(n, list(s), d))
        ps = lambda n, s, d=F32: es.enter_context(nc.psum_tensor(n, list(s), d))

        hbf = sb("hbf", [128, CH * T], BF16)
        ba = sb("ba", [128, CH * T], BF16)
        wst_s = sb("wst", [128, 2048], F32)
        wst = [wst_s, wst_s]
        wbf = [sb(f"wbf{i}", [128, 2048], BF16) for i in range(2)]
        tmpf = sb("tmpf", [128, 1024], F32)
        rows = sb("rows", [1, 1024], F32)
        rows2 = sb("rows2", [1, 1024], F32)
        xc_r = [sb(f"xc{i}", [128, 512], BF16) for i in range(2)]
        sq_r = [sb(f"sq{i}", [128, 512], BF16) for i in range(2)]
        pbf = [sb(f"pbf{i}", [128, 512], BF16) for i in range(2)]
        identf = sb("identf", [128, 128], F32)
        identb = sb("identb", [128, 128], BF16)
        onesr = sb("onesr", [1, 128], F32)
        onescb = sb("onescb", [128, 1], BF16)
        srow2 = sb("srow2", [1, 512], F32)
        ccol = sb("ccol", [128, 4], BF16)
        modT = sb("modT", [96, 128], F32)
        mod_pp = sb("mod_pp", [128, 192], F32)
        bias_pp = sb("bias_pp", [128, 36], F32)
        nw_pp = sb("nw_pp", [128, 16], F32)
        ltok = sb("ltok", [128, 64], F32)
        scol = sb("scol", [128, 24], F32)

        es_bc = contextlib.ExitStack()
        bc = sbx(es_bc, "bc", [128, CH * T], BF16)
        kTm = bc[:, 0:T]
        vm = bc[:, 8192:8192 + T]

        es_x = contextlib.ExitStack()
        x_res = sbx(es_x, "x_res", [128, CH * T], F32)

        es_l0 = contextlib.ExitStack()
        mask_sb = sbx(es_l0, "mask_sb", [128, 512], BF16)
        expp = [sbx(es_l0, f"expp{i}", [128, 512], BF16) for i in range(2)]
        bvb = sbx(es_l0, "bvb", [128, 128], F32)
        Iq = sbx(es_l0, "Iq", [128, 512], F32)
        coffc = sbx(es_l0, "coffc", [128, 128], F32)
        rbc = sbx(es_l0, "rbc", [128, 512], F32)

        PA = ps("PA", [128, 1024])
        PB = ps("PB", [128, 1024])
        PC = ps("PC", [128, 1024])
        PD = ps("PD", [128, 512])
        PTb = ps("PTb", [128, 1024], BF16)

        S = Ctx(nc, es)
        block = es.enter_context(nc.Block())

        def secT(fn):
            @block.tensor
            def _(te):
                fn(te)

        def secV(fn):
            @block.vector
            def _(v):
                fn(v)

        def secS(fn):
            @block.scalar
            def _(sc):
                fn(sc)

        def secG(fn):
            @block.gpsimd
            def _(g):
                fn(g)

        def secQ(fn):
            @block.sync
            def _(q):
                fn(q)

        def hch(buf, c, w=T):
            return buf[:, c * w:(c + 1) * w]

        def mmW(te, out, lhsT, rhs, start, stop):
            te.matmul(out[:, 0:512], lhsT, rhs[:, 0:512], start=start, stop=stop)
            return te.matmul(out[:, 512:1024], lhsT, rhs[:, 512:1024],
                             start=start, stop=stop)

        wchain = {"c": None}
        wguard = {0: None, 1: None}


        def cut_dram(*toks):
            def outst_d(q):
                S.wait(q, "sync", *toks)
                for c in range(CH):
                    S.inc(q.dma_start(out=outT[c:c + 1], in_=xsp[c:c + 1]),
                          "qS", 16)
                S.wait(q, "sync", S.now("qS"))
            secQ(outst_d)

        def cut_out(tok):
            def outst_c(q):
                S.wait(q, "sync", tok)
                for c in range(CH):
                    S.inc(q.dma_start(out=outT[c:c + 1], in_=hch(x_res, c)),
                          "qS", 16)
                S.wait(q, "sync", S.now("qS"))
            secQ(outst_c)
        def ph0_q(q):
            for c in range(CH):
                S.inc(q.dma_start(out=hch(x_res, c), in_=xT[c:c + 1]), "qS", 16)
            for dst, src in ((bias_pp[:, 0:16], bq), (bias_pp[:, 16:17], bk),
                             (bias_pp[:, 17:33], bfc), (nw_pp[:, :], nw1),
                             (coffc[0:1, :], coff)):
                S.inc(q.dma_start(out=dst, in_=src), "qS", 16)
            for lyr in range(2):
                for kc in range(2):
                    S.inc(q.dma_start(
                        out=srow2[0:1, 256 * lyr + 128 * kc:
                                  256 * lyr + 128 * (kc + 1)]
                        .rearrange("l (o t) -> l o t", o=1),
                        in_=c01[lyr:lyr + 1, kc:kc + 1, :]), "qS", 16)
            S.inc(q.dma_start(out=bvb[0:1, :], in_=bv), "qS", 16)
        secQ(ph0_q)
        tok_x = S.now("qS")
        tok_sml = S.now("qS")
        wchain["c"] = tok_sml

        def ph0_g(g):
            S.inc(g.memset(onesr[:, :], 1.0), "G")
            S.inc(g.memset(onescb[:, :], 1.0), "G")
            S.inc(g.memset(identf[:, :], 0.0), "G")
            S.wait(g, "gpsimd", S.now("G"))
            S.inc(g.affine_select(out=identf[:, :], in_=identf[:, :],
                                  compare_op=ALU.not_equal, fill=1.0, base=0,
                                  pattern=[[-1, 128]], channel_multiplier=1), "G")
            S.inc(g.iota(Iq[:, :], pattern=[[1, 512]], base=0, channel_multiplier=-1,
                         allow_small_or_imprecise_dtypes=True), "G")
        secG(ph0_g)
        tok_consts = S.now("G")

        def ph0_v(v):
            S.wait(v, "vector", tok_consts)
            S.inc(v.tensor_copy(identb[:, :], identf[:, :]), "V")
        secV(ph0_v)
        tok_identb = S.now("V")

        def ph1_s(sc):
            S.wait(sc, "scalar", tok_sml)
            S.inc(sc.activation(srow2[:, :], srow2[:, :], ACT.Silu), "S")
        secS(ph1_s)
        tok_siluc = S.now("S")

        def ph1_t2(te):
            S.wait(te, "tensor", tok_siluc, tok_consts)
            i = None
            for lyr in range(2):
                for kc in range(2):
                    i = te.transpose(PD[:, 2 * lyr + kc:2 * lyr + kc + 1],
                                     srow2[0:1, 256 * lyr + 128 * kc:
                                           256 * lyr + 128 * (kc + 1)],
                                     identf[0:1, 0:1])
            S.inc(i, "T")
        secT(ph1_t2)
        tok_ct = S.now("T")

        def ph1_v(v):
            S.wait(v, "vector", tok_ct)
            S.inc(v.tensor_copy(ccol[:, :], PD[:, 0:4]), "V")
        secV(ph1_v)
        tok_ccol = S.now("V")

        cast_m = {}
        load_m = {}
        mm_m = {}
        ev_m_tok = {}
        stq_m = {}
        for lyr in range(2):
            for pi in range(12):
                j = lyr * 12 + pi
                par = j % 2

                def ld(q, lyr=lyr, pi=pi, par=par, j=j, pc=wchain["c"]):
                    S.wait(q, "sync", pc)
                    for kc in range(2):
                        i = q.dma_start(
                            out=wst[par][:, 1024 * kc:1024 * (kc + 1)],
                            in_=wmod[lyr:lyr + 1, kc:kc + 1, :,
                                     1024 * pi:1024 * (pi + 1)])
                        S.inc(i, "qS", 16)
                    load_m[j] = S.now("qS")
                secQ(ld)

                def cst(g, j=j, par=par, pg=wguard[par]):
                    S.wait(g, "gpsimd", load_m[j], pg)
                    cast_m[j] = S.inc(g.tensor_copy(wbf[par][:, :], wst[par][:, :]), "G")
                secG(cst)
                wchain["c"] = cast_m[j]

                def mm(te, lyr=lyr, par=par, j=j):
                    S.wait(te, "tensor", cast_m[j], tok_ccol, ev_m_tok.get(j - 1))
                    i = None
                    for nt in range(2):
                        for kc in range(2):
                            i = te.matmul(PA[0:1, 512 * nt:512 * (nt + 1)],
                                          ccol[:, 2 * lyr + kc:2 * lyr + kc + 1],
                                          wbf[par][:, 1024 * kc + 512 * nt:
                                                   1024 * kc + 512 * (nt + 1)],
                                          start=(kc == 0), stop=(kc == 1))
                    mm_m[j] = S.inc(i, "T")
                secT(mm)
                wguard[par] = mm_m[j]

                def ev(v, j=j):
                    S.wait(v, "vector", mm_m[j], stq_m.get(j - 1))
                    ev_m_tok[j] = S.inc(v.tensor_copy(rows[0:1, :], PA[0:1, :]), "V")
                secV(ev)

                def stq(g, lyr=lyr, pi=pi, j=j):
                    S.wait(g, "gpsimd", ev_m_tok[j])
                    i = g.dma_start(out=mod_in[lyr:lyr + 1, 1024 * pi:1024 * (pi + 1)],
                                    in_=rows[0:1, :])
                    S.inc(i, "qG", 16)
                    stq_m[j] = S.now("qG")
                secG(stq)

        def ph1_ar(g):
            S.wait(g, "gpsimd", stq_m[23])
            i = g.collective_compute("AllReduce", ALU.add,
                                     replica_groups=[list(range(NCORE))],
                                     ins=[mod_in], outs=[mod_out])
            S.inc(i, "CC")
        secG(ph1_ar)
        tok_modar = S.now("CC")

        prev_mtr = [None]
        for lyr in range(2):
            def mload(g, lyr=lyr, pg=prev_mtr[0]):
                S.wait(g, "gpsimd", tok_modar, pg)
                i = g.dma_start(out=modT[:, :],
                                in_=mod_out.rearrange("l (r p) -> l r p", p=128)[lyr:lyr + 1])
                S.inc(i, "qG", 16)
            secG(mload)
            tok_ml = S.now("qG")
            tok_pbfree = S.now("V")

            def mtr(te, tok_ml=tok_ml, tok_pbfree=tok_pbfree):
                S.wait(te, "tensor", tok_ml, tok_consts, tok_pbfree)
                S.inc(te.transpose(PB[:, 0:96], modT[0:96, :], identf[0:96, 0:96]), "T")
            secT(mtr)
            tok_mt = S.now("T")
            prev_mtr[0] = tok_mt

            def mpost(v, lyr=lyr, tok_mt=tok_mt):
                S.wait(v, "vector", tok_mt)
                o = 96 * lyr
                S.inc(v.tensor_copy(mod_pp[:, o:o + 96], PB[:, 0:96]), "V")
                S.wait(v, "vector", S.now("V"))
                S.inc(v.tensor_scalar(mod_pp[:, o + 16:o + 32], mod_pp[:, o + 16:o + 32],
                                      1.0, None, ALU.add), "V")
                S.inc(v.tensor_scalar(mod_pp[:, o + 64:o + 80], mod_pp[:, o + 64:o + 80],
                                      1.0, None, ALU.add), "V")
                for base in (o + 32, o + 80):
                    sl = mod_pp[:, base:base + 16]
                    S.wait(v, "vector", S.now("V"))
                    S.inc(v.tensor_scalar(scol[:, 8:24], sl, 0.0, None, ALU.is_ge), "V")
                    S.wait(v, "vector", S.now("V"))
                    S.inc(v.tensor_scalar(scol[:, 8:24], scol[:, 8:24], 2.0, -1.0,
                                          ALU.mult, ALU.add), "V")
                    S.wait(v, "vector", S.now("V"))
                    S.inc(v.tensor_mul(sl, sl, scol[:, 8:24]), "V")
                    S.wait(v, "vector", S.now("V"))
                    S.inc(v.tensor_scalar(sl, sl, 0.25, None, ALU.max), "V")
                    S.wait(v, "vector", S.now("V"))
                    S.inc(v.tensor_mul(sl, sl, scol[:, 8:24]), "V")
            secV(mpost)
        tok_mod = S.now("V")

        def emit_ln(lyr, second, extra=()):
            oA = 96 * lyr + (64 if second else 16)
            oB = 96 * lyr + (48 if second else 0)
            tok_pdfree = S.now("V")
            tok_out = None
            for tq in range(2):
                t0 = tq * 512
                cast_t = {}
                sq_t = {}
                mm_t = {}
                for c in range(CH):
                    def cst(v, c=c, t0=t0):
                        S.wait(v, "vector", tok_x, tok_mod, tok_pdfree, *extra)
                        i = v.tensor_copy(xc_r[c % 2][:, :],
                                          x_res[:, c * T + t0:c * T + t0 + 512])
                        cast_t[c] = S.inc(i, "V")
                    secV(cst)

                    def sqr(sc, c=c, t0=t0):
                        S.wait(sc, "scalar", tok_x, tok_mod, *extra)
                        i = sc.activation(sq_r[c % 2][:, :],
                                          x_res[:, c * T + t0:c * T + t0 + 512],
                                          ACT.Square)
                        sq_t[c] = S.inc(i, "S")
                    secS(sqr)

                    def mms(te, c=c):
                        S.wait(te, "tensor", cast_t[c], sq_t[c], tok_identb)
                        te.matmul(PD[0:1, :], onescb[:, :], xc_r[c % 2][:, :],
                                  start=(c == 0), stop=(c == CH - 1))
                        i = te.matmul(PD[64:65, :], onescb[:, :], sq_r[c % 2][:, :],
                                      start=(c == 0), stop=(c == CH - 1))
                        mm_t[c] = S.inc(i, "T")
                    secT(mms)
                    if c < CH - 1:
                        def rgv(v, c=c):
                            S.wait(v, "vector", mm_t[c])
                        def rgs(sc, c=c):
                            S.wait(sc, "scalar", mm_t[c])
                        secV(rgv)
                        secS(rgs)
                tok_sums = S.now("T")

                def stats(v):
                    S.wait(v, "vector", tok_sums)
                    S.inc(v.tensor_scalar(rows[0:1, 0:512], PD[0:1, :], 1.0 / D, None,
                                          ALU.mult), "V")
                    S.inc(v.tensor_scalar(rows[0:1, 512:1024], PD[64:65, :], 1.0 / D,
                                          None, ALU.mult), "V")
                    S.wait(v, "vector", S.now("V"))
                    S.inc(v.tensor_mul(rows2[0:1, 0:512], rows[0:1, 0:512],
                                       rows[0:1, 0:512]), "V")
                    S.wait(v, "vector", S.now("V"))
                    S.inc(v.tensor_sub(rows2[0:1, 0:512], rows[0:1, 512:1024],
                                       rows2[0:1, 0:512]), "V")
                    S.wait(v, "vector", S.now("V"))
                    S.inc(v.tensor_scalar(rows2[0:1, 0:512], rows2[0:1, 0:512],
                                          LN_EPS, None, ALU.add), "V")
                secV(stats)
                tok_var = S.now("V")

                def rstd(sc):
                    S.wait(sc, "scalar", tok_var)
                    S.inc(sc.activation(rows2[0:1, 512:1024], rows2[0:1, 0:512],
                                        ACT.Sqrt), "S")
                secS(rstd)
                tok_sqv = S.now("S")

                def rstdr(v):
                    S.wait(v, "vector", tok_sqv)
                    S.inc(v.reciprocal(rows2[0:1, 512:1024],
                                       rows2[0:1, 512:1024]), "V")
                secV(rstdr)
                tok_rstd = S.now("V")

                def mr(v):
                    S.wait(v, "vector", tok_rstd)
                    S.inc(v.tensor_mul(rows[0:1, 0:512], rows[0:1, 0:512],
                                       rows2[0:1, 512:1024]), "V")
                secV(mr)
                tok_mr = S.now("V")

                def bcasts(te):
                    S.wait(te, "tensor", tok_mr)
                    te.matmul(PA[:, 0:512], onesr[:, :], rows2[0:1, 512:1024],
                              start=True, stop=True)
                    i = te.matmul(PA[:, 512:1024], onesr[:, :], rows[0:1, 0:512],
                                  start=True, stop=True)
                    S.inc(i, "T")
                secT(bcasts)
                tok_bc = S.now("T")

                app_v_t = {}
                app_s_t = {}
                for c in range(CH):
                    def app_v(v, c=c, t0=t0):
                        S.wait(v, "vector", tok_bc, app_s_t.get(c - 2))
                        half = tmpf[:, 0:512] if c % 2 == 0 else tmpf[:, 512:1024]
                        xs = x_res[:, c * T + t0:c * T + t0 + 512]
                        i = v.tensor_mul(half, xs, PA[:, 0:512])
                        S.inc(i, "V")
                        S.wait(v, "vector", S.now("V"))
                        i = v.tensor_sub(half, half, PA[:, 512:1024])
                        app_v_t[c] = S.inc(i, "V")
                    secV(app_v)

                    def app_s(v, c=c, t0=t0):
                        S.wait(v, "vector", app_v_t[c])
                        half = tmpf[:, 0:512] if c % 2 == 0 else tmpf[:, 512:1024]
                        i = v.tensor_scalar(hbf[:, c * T + t0:c * T + t0 + 512],
                                            half, mod_pp[:, oA + c:oA + c + 1],
                                            mod_pp[:, oB + c:oB + c + 1],
                                            ALU.mult, ALU.add)
                        app_s_t[c] = S.inc(i, "V")
                    secV(app_s)
                tok_out = app_s_t[CH - 1]
            return tok_out

        if trunc <= 1:
            cut_out(tok_mod)
            es_l0.close(); es_x.close(); es_bc.close()
            return nc
        tok_ln1 = emit_ln(0, False)

        def stream_w(dram4, nm, kc, consume, rtok, mw=128):
            cast_tok = {}
            free_tok = {}
            load_tok = {}
            for j in range(nm):
                par = j % 2

                def ld(q, j=j, par=par, pc=wchain["c"]):
                    S.wait(q, "sync", pc)
                    i = q.dma_start(
                        out=wst[par][:, 0:kc * mw].rearrange("p (k m) -> p k m", k=kc),
                        in_=dram4[j:j + 1])
                    S.inc(i, "qS", 16)
                    load_tok[j] = S.now("qS")
                secQ(ld)

                def cst(g, j=j, par=par, pg=wguard[par]):
                    S.wait(g, "gpsimd", load_tok[j], pg)
                    i = g.tensor_copy(wbf[par][:, 0:kc * mw], wst[par][:, 0:kc * mw])
                    cast_tok[j] = S.inc(i, "G")
                secG(cst)
                wchain["c"] = cast_tok[j]

                free_tok[j] = consume(j, wbf[par], cast_tok[j], rtok)
                wguard[par] = free_tok[j]

        def bv_bc(te):
            S.wait(te, "tensor", tok_sml, tok_consts)
            S.inc(te.matmul(PC[:, 0:128], onesr[:, :], bvb[0:1, :],
                            start=True, stop=True), "T")
        secT(bv_bc)
        tok_bvmm = S.now("T")

        def bv_cp(v):
            S.wait(v, "vector", tok_bvmm)
            S.inc(v.tensor_copy(bvb[:, :], PC[:, 0:128]), "V")
        secV(bv_cp)
        tok_bvb = S.now("V")

        def coffbc(te):
            S.wait(te, "tensor", tok_sml, tok_consts)
            S.inc(te.matmul(PC[:, 128:256], onesr[:, :], coffc[0:1, :],
                            start=True, stop=True), "T")
        secT(coffbc)
        tok_cbc = S.now("T")

        def coffcp(v):
            S.wait(v, "vector", tok_cbc)
            S.inc(v.tensor_copy(coffc[:, :], PC[:, 128:256]), "V")
        secV(coffcp)
        tok_coff = S.now("V")

        qkv_ev = {}

        def l0qkv_consume(j, wb, ctok, rtok):
            if j < 16:
                pq = PA if j % 2 == 0 else PB

                def mmf(te, j=j, wb=wb, ctok=ctok, pq=pq):
                    S.wait(te, "tensor", ctok, rtok, qkv_ev.get(j - 2))
                    i = None
                    for k in range(16):
                        i = mmW(te, pq, wb[:, 128 * k:128 * (k + 1)],
                                hch(hbf, k), (k == 0), (k == 15))
                    S.inc(i, "T")
                secT(mmf)
                tok_mm = S.now("T")

                def evf(v, j=j, pq=pq, tok_mm=tok_mm):
                    S.wait(v, "vector", tok_mm, tok_sml)
                    i = v.tensor_scalar(hch(ba, j), pq[:, :], bias_pp[:, j:j + 1],
                                        None, ALU.add)
                    qkv_ev[j] = S.inc(i, "V")
                secV(evf)
                return tok_mm
            if j == 16:
                def mmf(te, wb=wb, ctok=ctok):
                    S.wait(te, "tensor", ctok, rtok, qkv_ev.get(14))
                    i = None
                    for k in range(16):
                        i = mmW(te, PA, wb[:, 128 * k:128 * (k + 1)],
                                hch(hbf, k), (k == 0), (k == 15))
                    S.inc(i, "T")
                secT(mmf)
                tok_mm = S.now("T")

                def evf(v, tok_mm=tok_mm):
                    S.wait(v, "vector", tok_mm, tok_sml)
                    i = v.tensor_scalar(kTm[:, :], PA[:, :], bias_pp[:, 16:17],
                                        None, ALU.add)
                    qkv_ev[16] = S.inc(i, "V")
                secV(evf)
                return tok_mm
            def mmv(te, wb=wb, ctok=ctok):
                S.wait(te, "tensor", ctok, rtok, qkv_ev.get(15))
                i = None
                for tt in range(8):
                    for k in range(16):
                        i = te.matmul(PB[:, 128 * tt:128 * (tt + 1)],
                                      hch(hbf, k)[:, 128 * tt:128 * (tt + 1)],
                                      wb[:, 128 * k:128 * (k + 1)],
                                      start=(k == 0), stop=(k == 15))
                S.inc(i, "T")
            secT(mmv)
            tok_mm = S.now("T")

            def evv(v, tok_mm=tok_mm):
                S.wait(v, "vector", tok_mm, tok_bvb)
                i = None
                for tt in range(8):
                    i = v.tensor_add(vm[:, 128 * tt:128 * (tt + 1)],
                                     PB[:, 128 * tt:128 * (tt + 1)], bvb[:, :])
                qkv_ev[17] = S.inc(i, "V")
            secV(evv)
            return tok_mm

        stream_w(wqkv0, 18, 16, l0qkv_consume, tok_ln1)
        tok_qkv0 = S.now("V")

        def ag0(g):
            S.wait(g, "gpsimd", tok_qkv0)
            S.inc(g.dma_start(out=ag0_in[0:128, :], in_=kTm[:, :]), "qG", 16)
            S.inc(g.dma_start(out=ag0_in[128:256, :], in_=vm[:, :]), "qG", 16)
            S.wait(g, "gpsimd", S.now("qG"))
            i = g.collective_compute("AllGather", ALU.bypass,
                                     replica_groups=[list(range(NCORE))],
                                     ins=[ag0_in], outs=[ag0_out])
            S.inc(i, "CC")
            S.wait(g, "gpsimd", S.now("CC"))
            for cp in range(NCORE):
                S.inc(g.dma_start(out=bc[:, 1024 * cp:1024 * (cp + 1)],
                                  in_=ag0_out[256 * cp:256 * cp + 128, :]), "qG", 16)
                S.inc(g.dma_start(out=bc[:, 8192 + 1024 * cp:8192 + 1024 * (cp + 1)],
                                  in_=ag0_out[256 * cp + 128:256 * (cp + 1), :]),
                      "qG", 16)
        secG(ag0)
        tok_kv = S.now("qG")

        groups = [(0, 1, 2), (3, 4, 5), (6, 7, 8), (9, 10, 11), (12, 13, 14), (15,)]
        pvs = [PA[:, 0:512], PA[:, 512:1024], PB[:, 0:512]]
        spp = [PB[:, 512:1024], PC[:, 0:512]]
        rsoff = [0, 32, 64]

        for qt in range(2):
            for grp in groups:
                gsz = len(grp)
                bT = S.c["T"]
                bS = S.c["S"]
                bV = S.c["V"]
                N = 64 * gsz

                def cnt_S(n):
                    return bT + (n + 1 if n < 2 else 3 * n - 1)

                def cnt_RS(n):
                    return bT + 3 * n + 4

                def cnt_exp(n):
                    return bS + n + 1

                def cnt_mult(n):
                    return bV + n // gsz + n + 2

                def ph3_v(v, qt=qt, gsz=gsz, N=N, bT=bT, bS=bS,
                          cnt_exp=cnt_exp, cnt_RS=cnt_RS):
                    S.wait(v, "vector", tok_qkv0, tok_coff, tok_consts)
                    v.wait_ge(S.h["T"], bT)
                    for t in range(64):
                        v.wait_ge(S.h["V"], S.c["V"])
                        i = v.tensor_scalar(mask_sb[:, :], Iq[:, :],
                                            coffc[:, 64 * qt + t:64 * qt + t + 1], 0.0,
                                            ALU.add, ALU.is_ge)
                        S.c["V"] += 1
                        i.then_inc(S.h["V"], 1)
                        v.wait_ge(S.h["V"], S.c["V"])
                        for idx in range(gsz):
                            n = t * gsz + idx
                            v.wait_ge(S.h["S"], cnt_exp(n))
                            if n >= 2:
                                v.wait_ge(S.h["T"], cnt_RS(n - 2))
                            i = v.tensor_mul(pbf[n % 2][:, :], expp[n % 2][:, :],
                                             mask_sb[:, :])
                            S.c["V"] += 1
                            i.then_inc(S.h["V"], 1)
                secV(ph3_v)

                def ph3_s(sc, N=N, bV=bV, cnt_S=cnt_S, cnt_mult=cnt_mult):
                    sc.wait_ge(S.h["V"], bV)
                    for n in range(N):
                        sc.wait_ge(S.h["T"], cnt_S(n))
                        if n >= 2:
                            sc.wait_ge(S.h["V"], cnt_mult(n - 2))
                        i = sc.activation(expp[n % 2][:, :], spp[n % 2][:, :], ACT.Exp,
                                          scale=ISQ)
                        S.c["S"] += 1
                        i.then_inc(S.h["S"], 1)
                secS(ph3_s)

                def ph3_t(te, qt=qt, grp=grp, gsz=gsz, N=N, bT=bT, bS=bS, bV=bV,
                          cnt_exp=cnt_exp, cnt_mult=cnt_mult):
                    S.wait(te, "tensor", tok_kv, tok_qkv0, tok_identb)
                    te.wait_ge(S.h["V"], bV)
                    te.wait_ge(S.h["S"], bS)

                    def emit_S(n):
                        t, idx = divmod(n, gsz)
                        h = grp[idx]
                        if n >= 2:
                            te.wait_ge(S.h["S"], cnt_exp(n - 2))
                        i = te.matmul(spp[n % 2],
                                      bc[:, 128 * t:128 * (t + 1)],
                                      ba[:, 1024 * h + 512 * qt:
                                          1024 * h + 512 * (qt + 1)],
                                      start=True, stop=True)
                        S.c["T"] += 1
                        i.then_inc(S.h["T"], 1)

                    emit_S(0)
                    if N > 1:
                        emit_S(1)
                    for n in range(N):
                        t, idx = divmod(n, gsz)
                        te.wait_ge(S.h["V"], cnt_mult(n))
                        i = te.matmul(pvs[idx],
                                      bc[:, 8192 + 128 * t:8192 + 128 * (t + 1)],
                                      pbf[n % 2][:, :], start=(t == 0), stop=(t == 63))
                        S.c["T"] += 1
                        i.then_inc(S.h["T"], 1)
                        i = te.matmul(PD[rsoff[idx]:rsoff[idx] + 1, :], onescb[:, :],
                                      pbf[n % 2][:, :], start=(t == 0), stop=(t == 63))
                        S.c["T"] += 1
                        i.then_inc(S.h["T"], 1)
                        if n + 2 < N:
                            emit_S(n + 2)
                secT(ph3_t)
                tok_grp_T = S.now("T")

                for idx, h in enumerate(grp):
                    def ev_r(v, idx=idx, tok_grp_T=tok_grp_T):
                        S.wait(v, "vector", tok_grp_T)
                        S.inc(v.reciprocal(rows2[0:1, 0:512],
                                           PD[rsoff[idx]:rsoff[idx] + 1, :]), "V")
                    secV(ev_r)
                    tok_recip = S.now("V")

                    def ev_b(te, tok_recip=tok_recip):
                        S.wait(te, "tensor", tok_recip)
                        S.inc(te.matmul(PC[:, 512:1024], onesr[:, :],
                                        rows2[0:1, 0:512], start=True, stop=True), "T")
                    secT(ev_b)
                    tok_bcv = S.now("T")

                    def ev_m(v, idx=idx, h=h, qt=qt, tok_bcv=tok_bcv):
                        S.wait(v, "vector", tok_bcv)
                        S.wait(v, "vector", S.now("V"))
                        S.inc(v.tensor_copy(rbc[:, :], PC[:, 512:1024]), "V")
                        S.wait(v, "vector", S.now("V"))
                        i = v.tensor_mul(hbf[:, 1024 * h + 512 * qt:
                                             1024 * h + 512 * (qt + 1)],
                                         pvs[idx], rbc[:, :])
                        S.inc(i, "V")
                    secV(ev_m)

        tok_attn0 = S.now("V")

        fc_ev = {}

        def fc_consume(j, wb, ctok, rtok):
            pq = PA if j % 2 == 0 else PB

            def mmf(te, j=j, wb=wb, ctok=ctok, pq=pq):
                S.wait(te, "tensor", ctok, rtok, fc_ev.get(j - 2))
                i = None
                for k in range(16):
                    i = mmW(te, pq, wb[:, 128 * k:128 * (k + 1)],
                            hch(hbf, k), (k == 0), (k == 15))
                S.inc(i, "T")
            secT(mmf)
            tok_mm = S.now("T")

            def evf(v, j=j, pq=pq, tok_mm=tok_mm):
                S.wait(v, "vector", tok_mm, tok_mod)
                S.wait(v, "vector", S.now("V"))
                i = v.tensor_scalar(tmpf[:, :], pq[:, :], bias_pp[:, 17 + j:18 + j],
                                    mod_pp[:, 32 + j:33 + j], ALU.add, ALU.mult)
                S.inc(i, "V")
                S.wait(v, "vector", S.now("V"))
                i = v.tensor_add(hch(x_res, j), hch(x_res, j), tmpf[:, :])
                fc_ev[j] = S.inc(i, "V")
            secV(evf)
            return tok_mm

        stream_w(wfc, 16, 16, fc_consume, tok_attn0)
        tok_x1 = S.now("V")

        if DEBUG_DUMPS:
            def dmp1(g, tok_x1=tok_x1):
                S.wait(g, "gpsimd", tok_x1)
                for c in range(CH):
                    S.inc(g.dma_start(out=xd1[c:c + 1], in_=hch(x_res, c)),
                          "qG", 16)
            secG(dmp1)
            tok_x1 = S.now("qG")

        if trunc <= 3:
            cut_out(tok_x1)
            es_l0.close(); es_x.close(); es_bc.close()
            return nc
        es_l0.close()
        es_l0b = contextlib.ExitStack()
        exp_sel = sbx(es_l0b, "exp_sel", [8, 512], F32)
        m2t = sbx(es_l0b, "m2t", [8, 1024], F32)

        def ph5_sel_g(g):
            S.wait(g, "gpsimd", tok_attn0)
            S.inc(g.iota(exp_sel[:, :], pattern=[[1, 8], [0, 64]], base=0,
                         channel_multiplier=-1,
                         allow_small_or_imprecise_dtypes=True), "G")
        secG(ph5_sel_g)
        tok_sel_g = S.now("G")

        def ph5_sel_v(v):
            S.wait(v, "vector", tok_sel_g)
            S.inc(v.tensor_scalar(exp_sel[:, :], exp_sel[:, :], 0.0, None,
                                  ALU.is_equal), "V")
        secV(ph5_sel_v)

        tok_ln2 = emit_ln(0, True, extra=(tok_x1,))

        def wg_consume(j, wb, ctok, rtok):
            def mmf(te, wb=wb, ctok=ctok):
                S.wait(te, "tensor", ctok, rtok)
                i = None
                for k in range(16):
                    i = mmW(te, PA[0:8, :], wb[:, 8 * k:8 * (k + 1)],
                            hch(hbf, k), (k == 0), (k == 15))
                S.inc(i, "T")
            secT(mmf)
            return S.now("T")

        stream_w(wg_t, 1, 16, wg_consume, tok_ln2, mw=8)
        tok_lg = S.now("T")

        def lg_cp(v):
            S.wait(v, "vector", tok_lg)
            S.inc(v.tensor_copy(m2t[:, :], PA[0:8, :]), "V")
        secV(lg_cp)
        tok_lgc = S.now("V")

        def lg_tr(te):
            S.wait(te, "tensor", tok_lgc)
            i = None
            for tt in range(8):
                i = te.transpose(PD[:, 8 * tt:8 * tt + 8],
                                 m2t[0:8, 128 * tt:128 * (tt + 1)], identf[0:8, 0:8])
            S.inc(i, "T")
        secT(lg_tr)
        tok_ltr = S.now("T")

        def lg_tok(v):
            S.wait(v, "vector", tok_ltr)
            S.inc(v.tensor_copy(ltok[:, :], PD[:, 0:64]), "V")
        secV(lg_tok)
        tok_ltok = S.now("V")

        tr_toks = {}
        for tt in range(8):
            def amax(v, tt=tt):
                S.wait(v, "vector", tok_ltok)
                S.inc(v.reduce_max(scol[:, 0:1], ltok[:, 8 * tt:8 * (tt + 1)], axis=AX,
                                   negate=True), "V")
            secV(amax)
            tok_nm = S.now("V")

            def expx(sc, tt=tt, tok_nm=tok_nm):
                S.wait(sc, "scalar", tok_nm, tr_toks.get(tt - 1))
                i = sc.activation(scol[:, 2:10], ltok[:, 8 * tt:8 * (tt + 1)], ACT.Exp,
                                  bias=scol[:, 0:1], accum_out=scol[:, 1:2])
                S.inc(i, "S")
            secS(expx)
            tok_e = S.now("S")

            def mk(v, tt=tt, tok_e=tok_e):
                S.wait(v, "vector", tok_e)
                S.inc(v.reciprocal(scol[:, 1:2], scol[:, 1:2]), "V")
                S.inc(v.tensor_scalar(scol[:, 2:10], ltok[:, 8 * tt:8 * (tt + 1)],
                                      scol[:, 0:1], 0.0, ALU.add, ALU.is_equal), "V")
                S.wait(v, "vector", S.now("V"))
                S.inc(v.tensor_scalar(scol[:, 2:10], scol[:, 2:10], scol[:, 1:2], 4.0,
                                      ALU.mult, ALU.mult), "V")
            secV(mk)
            tok_m2 = S.now("V")

            def trb(te, tt=tt, tok_m2=tok_m2):
                S.wait(te, "tensor", tok_m2)
                i = te.transpose(PD[0:8, 128 * (tt % 4):128 * (tt % 4) + 128],
                                 scol[:, 2:10], identf[:, :])
                tr_toks[tt] = S.inc(i, "T")
            secT(trb)

            if tt % 4 == 3:
                def cpb(v, tt=tt):
                    S.wait(v, "vector", tr_toks[tt])
                    i = v.tensor_copy(m2t[0:8, 512 * (tt // 4):512 * (tt // 4 + 1)],
                                      PD[0:8, :])
                    S.inc(i, "V")
                secV(cpb)
        tok_m2t = S.now("V")

        lora_prev = {}

        def lora_consume(j, wb, ctok, rtok):
            def m3f(te, j=j, wb=wb, ctok=ctok):
                S.wait(te, "tensor", tok_m2t, ctok, rtok, lora_prev.get(j - 1))
                i = None
                for half in range(2):
                    i = te.matmul(PB[:, 512 * half:512 * (half + 1)],
                                  exp_sel[:, 128 * j:128 * (j + 1)],
                                  m2t[0:8, 512 * half:512 * (half + 1)],
                                  start=True, stop=True)
                for k in range(16):
                    i = mmW(te, PA, wb[:, 128 * k:128 * (k + 1)],
                            hch(hbf, k), (k == 0), (k == 15))
                S.inc(i, "T")
            secT(m3f)
            tok_mm = S.now("T")

            def hmf(v, j=j, tok_mm=tok_mm):
                S.wait(v, "vector", tok_mm)
                S.wait(v, "vector", S.now("V"))
                S.inc(v.tensor_copy(tmpf[:, :], PB[:, :]), "V")
                S.wait(v, "vector", S.now("V"))
                i = v.tensor_mul(hch(ba, j), PA[:, :], tmpf[:, :])
                lora_prev[j] = S.inc(i, "V")
            secV(hmf)
            return tok_mm

        stream_w(a_t, 4, 16, lora_consume, tok_ln2)
        tok_hm = S.now("V")

        bw_prev = {}

        def bw_consume(j, wb, ctok, rtok):
            pq = PA if j % 2 == 0 else PC

            def mmf(te, j=j, wb=wb, ctok=ctok, pq=pq):
                S.wait(te, "tensor", ctok, rtok, bw_prev.get(j - 2))
                i = None
                for k in range(4):
                    i = mmW(te, pq, wb[:, 128 * k:128 * (k + 1)],
                            hch(ba, k), (k == 0), (k == 3))
                S.inc(i, "T")
            secT(mmf)
            tok_mm = S.now("T")

            def evf(v, j=j, pq=pq, tok_mm=tok_mm):
                S.wait(v, "vector", tok_mm)
                i = v.scalar_tensor_tensor(hch(x_res, j), pq[:, :],
                                           mod_pp[:, 80 + j:81 + j], hch(x_res, j),
                                           ALU.mult, ALU.add)
                bw_prev[j] = S.inc(i, "V")
            secV(evf)
            return tok_mm

        stream_w(bw_f, 16, 4, bw_consume, tok_hm)
        tok_x2 = S.now("V")

        if trunc <= 5:
            cut_out(tok_x2)
            es_l0b.close(); es_x.close(); es_bc.close()
            return nc
        def spill_g(g):
            S.wait(g, "gpsimd", tok_x2)
            for c in range(CH):
                S.inc(g.dma_start(out=xsp[c:c + 1], in_=hch(x_res, c)), "qG", 16)
        secG(spill_g)
        tok_spill = S.now("qG")

        tok_ln3 = emit_ln(1, False, extra=(tok_x2,))
        tok_ln3v = S.now("V")

        es_l0b.close()
        es_x.close()
        es_l1 = contextlib.ExitStack()
        slr = sbx(es_l1, "slr", [1, 16], F32)
        evecb = sbx(es_l1, "evecb", [16, 32], F32)
        iot2 = sbx(es_l1, "iot2", [16, 1], F32)
        s_c = sbx(es_l1, "s_c", [16, 6], F32)
        Iar = sbx(es_l1, "Iar", [128, 256], F32)
        kdecP = sbx(es_l1, "kdecP", [16, 256], F32)
        kdecT = sbx(es_l1, "kdecT", [128, 32], F32)
        l256c = sbx(es_l1, "l256c", [128, 16], F32)
        snegc = sbx(es_l1, "snegc", [128, 16], F32)
        wtbl = sbx(es_l1, "wtbl", [16, 32], F32)
        wcols = sbx(es_l1, "wcols", [128, 512], F32)
        I2 = sbx(es_l1, "I2", [128, 512], F32)
        iotr = sbx(es_l1, "iotr", [16, 256], F32)
        dd = sbx(es_l1, "dd", [128, 512], F32)
        qT_h = sbx(es_l1, "qT_h", [128, T], BF16)
        qd_h = sbx(es_l1, "qd_h", [128, T], BF16)
        vtmp = sbx(es_l1, "vtmp", [128, T], BF16)
        kd_h = sbx(es_l1, "kd_h", [128, T], BF16)
        qdb = sbx(es_l1, "qdb", [128, 256], F32)
        ub0 = sbx(es_l1, "ub0", [128, 2048], F32)
        agu_sb = sbx(es_l1, "agu_sb", [128, 1024], F32)
        uld = [sbx(es_l1, f"uld{i}", [128, 512], F32) for i in range(2)]
        sst = sbx(es_l1, "sst", [128, 1024], F32)
        sbf = sbx(es_l1, "sbf", [128, 128], BF16)
        smid = sbx(es_l1, "smid", [128, 128], F32)
        gt_r = [sbx(es_l1, f"gt{i}", [128, 1024], BF16) for i in range(2)]

        def ph6_iota(g):
            S.wait(g, "gpsimd", tok_ln3, tok_spill)
            S.inc(g.iota(I2[:, :], pattern=[[-128, 2], [1, 256]], base=0,
                         channel_multiplier=-1,
                         allow_small_or_imprecise_dtypes=True), "G")
            S.inc(g.iota(iotr[:, :], pattern=[[1, 256]], base=1,
                         channel_multiplier=0,
                         allow_small_or_imprecise_dtypes=True), "G")
            S.inc(g.iota(iot2[:, :], pattern=[[0, 1]], base=256,
                         channel_multiplier=0,
                         allow_small_or_imprecise_dtypes=True), "G")
            S.inc(g.iota(Iar[:, :], pattern=[[1, 256]], base=1,
                         channel_multiplier=0,
                         allow_small_or_imprecise_dtypes=True), "G")
            S.inc(g.dma_start(out=slr[:, :], in_=slope), "qG", 16)
            for p in range(16):
                S.inc(g.dma_start(out=evecb[p:p + 1, :], in_=evec), "qG", 16)
        secG(ph6_iota)
        tok_iot = S.now("G")
        tok_sml6 = S.now("qG")

        if trunc == 40:
            cut_dram(tok_iot, tok_sml6, tok_ln3, tok_spill)
            es_l1.close(); es_bc.close()
            return nc

        def dt_tr(te):
            S.wait(te, "tensor", tok_sml6, tok_consts, tok_ln3v)
            S.inc(te.transpose(PD[0:16, 0:1], slr[:, :], identf[0:1, 0:1]), "T")
        secT(dt_tr)
        tok_str = S.now("T")

        def dt_v(v):
            S.wait(v, "vector", tok_str, tok_ln3, tok_spill)
            S.inc(v.tensor_copy(s_c[:, 0:1], PD[0:16, 0:1]), "V")
            S.wait(v, "vector", S.now("V"))
            S.inc(v.tensor_scalar(s_c[:, 1:2], s_c[:, 0:1], -1.0, None, ALU.mult), "V")
            S.inc(v.tensor_scalar(s_c[:, 2:3], s_c[:, 0:1], -512.0, None, ALU.mult), "V")
            S.inc(v.tensor_scalar(s_c[:, 3:4], s_c[:, 0:1], -256.0, None, ALU.mult), "V")
        secV(dt_v)
        tok_scol = S.now("V")

        def dt_s(sc):
            S.wait(sc, "scalar", tok_scol, tok_consts, tok_iot)
            S.inc(sc.activation(kdecP[:, :], iotr[:, :], ACT.Exp, scale=s_c[:, 0:1],
                                bias=s_c[:, 3:4]), "S")
            S.inc(sc.activation(s_c[:, 4:5], iot2[:, :], ACT.Exp, scale=s_c[:, 1:2]), "S")
        secS(dt_s)
        tok_dec1 = S.now("S")

        if trunc == 46:
            cut_dram(tok_dec1, tok_spill)
            es_l1.close(); es_bc.close()
            return nc

        tok_evb2 = tok_sml6

        if trunc == 48:
            cut_dram(tok_evb2, tok_spill)
            es_l1.close(); es_bc.close()
            return nc

        def wtbl_s(sc):
            S.wait(sc, "scalar", tok_evb2, tok_scol)
            S.inc(sc.activation(wtbl[:, :], evecb[:, :], ACT.Exp, scale=s_c[:, 2:3]), "S")
        secS(wtbl_s)
        tok_wtbl = S.now("S")

        if trunc == 49:
            cut_dram(tok_wtbl, tok_spill)
            es_l1.close(); es_bc.close()
            return nc

        def fl_st(g):
            S.wait(g, "gpsimd", tok_wtbl, tok_dec1)
            S.inc(g.dma_start(out=flat_w, in_=wtbl[:, :]), "qG", 16)
            S.inc(g.dma_start(out=flat_a, in_=s_c[:, 4:5]), "qG", 16)
            S.inc(g.dma_start(out=flat_b, in_=s_c[:, 1:2]), "qG", 16)
            S.wait(g, "gpsimd", S.now("qG"))
            S.inc(g.dma_start(out=rows[0:1, 0:512],
                              in_=flat_w.rearrange("(o a) b -> o (a b)", o=1)), "qG", 16)
            S.inc(g.dma_start(out=rows[0:1, 512:528],
                              in_=flat_a.rearrange("a b -> b a")), "qG", 16)
            S.inc(g.dma_start(out=rows[0:1, 528:544],
                              in_=flat_b.rearrange("a b -> b a")), "qG", 16)
        secG(fl_st)
        tok_fl = S.now("qG")

        if trunc == 47:
            cut_dram(tok_fl, tok_spill)
            es_l1.close(); es_bc.close()
            return nc

        def wc_bc(te):
            S.wait(te, "tensor", tok_fl)
            te.matmul(PC[:, 0:512], onesr[:, :], rows[0:1, 0:512], start=True, stop=True)
            i = te.matmul(PD[:, 0:32], onesr[:, :], rows[0:1, 512:544],
                          start=True, stop=True)
            S.inc(i, "T")
        secT(wc_bc)
        tok_wcb = S.now("T")

        def wc_cp(v):
            S.wait(v, "vector", tok_wcb)
            S.inc(v.tensor_copy(wcols[:, :], PC[:, 0:512]), "V")
            S.inc(v.tensor_copy(l256c[:, :], PD[:, 0:16]), "V")
            S.inc(v.tensor_copy(snegc[:, :], PD[:, 16:32]), "V")
        secV(wc_cp)
        tok_tabs = S.now("V")

        def kd_tr(te):
            S.wait(te, "tensor", tok_dec1, tok_identb, tok_tabs)
            te.transpose(PD[:, 64:80], kdecP[:, 0:128], identf[0:16, 0:16])
            i = te.transpose(PD[:, 80:96], kdecP[:, 128:256], identf[0:16, 0:16])
            S.inc(i, "T")
        secT(kd_tr)
        tok_kdt = S.now("T")

        def kd_cp(v):
            S.wait(v, "vector", tok_kdt)
            S.inc(v.tensor_copy(kdecT[:, 0:16], PD[:, 64:80]), "V")
            S.inc(v.tensor_copy(kdecT[:, 16:32], PD[:, 80:96]), "V")
        secV(kd_cp)
        tok_kdecT = S.now("V")
        if trunc == 41:
            cut_dram(tok_kdecT, tok_tabs, tok_dec1, tok_spill)
            es_l1.close(); es_bc.close()
            return nc

        l1_cast = {}
        l1_load = {}
        w_free = {}
        slot_ctr = [0]

        def l1_ld(j):
            s = slot_ctr[0]
            slot_ctr[0] += 1
            par = s % 2

            def ld(q, s=s, j=j, par=par, pc=wchain["c"]):
                S.wait(q, "sync", pc)
                i = q.dma_start(
                    out=wst[par][:, :].rearrange("p (k m) -> p k m", k=16),
                    in_=wqkv1[j:j + 1])
                S.inc(i, "qS", 16)
                l1_load[s] = S.now("qS")
            secQ(ld)

            def cst(g, s=s, par=par, pg=wguard[par]):
                S.wait(g, "gpsimd", l1_load[s], pg)
                i = g.tensor_copy(wbf[par][:, :], wst[par][:, :])
                l1_cast[s] = S.inc(i, "G")
            secG(cst)
            wchain["c"] = l1_cast[s]
            return s, par

        passB_done = {}
        agu_done = {}
        agu_dma = {}
        prev_silk = [tok_spill]
        prev_silv = [tok_spill]
        prev_kd = [tok_spill]
        prev_uev = [tok_spill]
        prev_silq = [tok_spill]
        prev_ddm = [tok_spill]
        prev_oev = [tok_spill]
        prev_qdv = [tok_spill]

        def emit_passA(g4):
            gpar = g4 % 2
            slot_guard = passB_done.get(g4 - 2)
            for h4 in range(4):
                h = 4 * g4 + h4
                kslab = ba[:, gpar * 8192 + h4 * 1024: gpar * 8192 + (h4 + 1) * 1024]
                vslab = ba[:, gpar * 8192 + 4096 + h4 * 1024:
                           gpar * 8192 + 4096 + (h4 + 1) * 1024]

                sk, park = l1_ld(16 + h)

                def mmk(te, sk=sk, park=park):
                    S.wait(te, "tensor", l1_cast[sk], tok_ln3, slot_guard,
                           prev_silk[0])
                    i = None
                    for k in range(16):
                        i = mmW(te, PA, wbf[park][:, 128 * k:128 * (k + 1)],
                                hch(hbf, k), (k == 0), (k == 15))
                    w_free[sk] = S.inc(i, "T")
                secT(mmk)
                wguard[park] = w_free[sk]
                tok_mmk = S.now("T")

                def silk(sc, kslab=kslab, tok_mmk=tok_mmk):
                    S.wait(sc, "scalar", tok_mmk)
                    i = sc.activation(kslab, PA[:, :], ACT.Silu)
                    prev_silk[0] = S.inc(i, "S")
                secS(silk)
                tok_k = S.now("S")

                sv, parv = l1_ld(32 + h)

                def mmv(te, sv=sv, parv=parv):
                    S.wait(te, "tensor", l1_cast[sv], prev_silv[0])
                    i = None
                    for k in range(16):
                        i = mmW(te, PB, wbf[parv][:, 128 * k:128 * (k + 1)],
                                hch(hbf, k), (k == 0), (k == 15))
                    w_free[sv] = S.inc(i, "T")
                secT(mmv)
                wguard[parv] = w_free[sv]
                tok_mmv = S.now("T")

                def silv(sc, tok_mmv=tok_mmv):
                    S.wait(sc, "scalar", tok_mmv)
                    i = sc.activation(vtmp[:, :], PB[:, :], ACT.Silu)
                    prev_silv[0] = S.inc(i, "S")
                secS(silv)
                tok_v = S.now("S")

                def trv(te, tok_v=tok_v):
                    S.wait(te, "tensor", tok_v, tok_identb, prev_kd[0])
                    i = None
                    for tt in range(8):
                        i = te.transpose(PTb[:, 128 * tt:128 * (tt + 1)],
                                         vtmp[:, 128 * tt:128 * (tt + 1)], identb[:, :])
                    S.inc(i, "T")
                secT(trv)
                tok_trv = S.now("T")

                def cpv(v, vslab=vslab, tok_trv=tok_trv):
                    S.wait(v, "vector", tok_trv)
                    S.inc(v.tensor_copy(vslab, PTb[:, :]), "V")
                secV(cpv)
                tok_vtok = S.now("V")

                def trk(te, kslab=kslab, tok_vtok=tok_vtok, tok_k=tok_k):
                    S.wait(te, "tensor", tok_vtok, tok_k)
                    i = None
                    for tt in range(8):
                        i = te.transpose(PTb[:, 128 * tt:128 * (tt + 1)],
                                         kslab[:, 128 * tt:128 * (tt + 1)], identb[:, :])
                    S.inc(i, "T")
                secT(trk)
                tok_trk = S.now("T")

                def cpk(v, h=h, tok_trk=tok_trk):
                    S.wait(v, "vector", tok_trk, tok_kdecT)
                    i = None
                    for tt in range(8):
                        i = v.tensor_scalar(kd_h[:, 128 * tt:128 * (tt + 1)],
                                            PTb[:, 128 * tt:128 * (tt + 1)],
                                            kdecT[:, 16 * (tt % 2) + h - 16 * (tt % 2) * 0:
                                                  16 * (tt % 2) + h + 1 - 16 * (tt % 2) * 0]
                                            if False else
                                            kdecT[:, 16 * (tt % 2) + h:
                                                  16 * (tt % 2) + h + 1],
                                            None, ALU.mult)
                    prev_kd[0] = S.inc(i, "V")
                secV(cpk)
                tok_kd = S.now("V")

                def umm(te, tok_kd=tok_kd, tok_vtok=tok_vtok, vslab=vslab):
                    S.wait(te, "tensor", tok_kd, prev_uev[0])
                    i = None
                    for b in range(4):
                        for sub in range(2):
                            tt = 2 * b + sub
                            i = te.matmul(PD[:, 128 * b:128 * (b + 1)],
                                          kd_h[:, 128 * tt:128 * (tt + 1)],
                                          vslab[:, 128 * tt:128 * (tt + 1)],
                                          start=(sub == 0), stop=(sub == 1))
                    S.inc(i, "T")
                secT(umm)
                tok_u = S.now("T")

                def uev(v, h4=h4, h=h, gpar=gpar, tok_u=tok_u):
                    S.wait(v, "vector", tok_u, tok_tabs, agu_dma.get(g4 - 1),
                           passB_done.get(g4 - 2))
                    i = None
                    for c in range(2):
                        u0 = ub0[:, gpar * 1024 + (c * 4 + h4) * 128:
                                 gpar * 1024 + (c * 4 + h4) * 128 + 128]
                        i = v.tensor_copy(u0, PD[:, 256 * c:256 * c + 128])
                        S.inc(i, "V")
                        S.wait(v, "vector", S.now("V"))
                        i = v.scalar_tensor_tensor(
                            agu_sb[:, (c * 4 + h4) * 128:(c * 4 + h4) * 128 + 128],
                            u0, l256c[:, h:h + 1],
                            PD[:, 256 * c + 128:256 * c + 256], ALU.mult, ALU.add)
                        prev_uev[0] = S.inc(i, "V")
                secV(uev)
            tok_passA = S.now("V")

            def ag_u(g, g4=g4, tok_passA=tok_passA):
                S.wait(g, "gpsimd", tok_passA)
                S.inc(g.dma_start(out=agu_in[g4], in_=agu_sb[:, :]), "qG", 16)
                agu_dma[g4] = S.now("qG")
                S.wait(g, "gpsimd", S.now("qG"))
                i = g.collective_compute("AllGather", ALU.bypass,
                                         replica_groups=[list(range(NCORE))],
                                         ins=[agu_in[g4]], outs=[agu_out[g4]])
                S.inc(i, "CC")
            secG(ag_u)
            agu_done[g4] = S.now("CC")

        def emit_passB(g4):
            gpar = g4 % 2
            tok_ag = agu_done[g4]
            uld_tok = {}
            acc_tok = {}
            for j in range(16):
                def uldma(g, j=j, g4=g4, tok_ag=tok_ag):
                    S.wait(g, "gpsimd", tok_ag, acc_tok.get(j - 2))
                    i = g.dma_start(out=uld[j % 2][:, :],
                                    in_=agu_out[g4][128 * (j // 2):128 * (j // 2) + 128,
                                                    512 * (j % 2):512 * (j % 2) + 512])
                    sm = "qU" if j % 2 == 0 else "qV"
                    S.inc(i, sm, 16)
                    uld_tok[j] = S.now(sm)
                secG(uldma)

                def accf(v, j=j, g4=g4):
                    S.wait(v, "vector", uld_tok[j], tok_tabs, acc_tok.get(j - 1))
                    i = None
                    for c in range(2):
                        for h4 in range(4):
                            h = 4 * g4 + h4
                            dst = sst[:, (c * 4 + h4) * 128:(c * 4 + h4) * 128 + 128]
                            src = uld[j % 2][:, 128 * h4:128 * (h4 + 1)]
                            wap = wcols[:, 32 * h + 16 * c + j:32 * h + 16 * c + j + 1]
                            if j == 0:
                                i = v.tensor_scalar(dst, src, wap, None, ALU.mult)
                            else:
                                i = v.scalar_tensor_tensor(dst, src, wap, dst,
                                                           ALU.mult, ALU.add)
                            S.inc(i, "V")
                    acc_tok[j] = S.now("V")
                secV(accf)
            tok_sst = S.now("V")

            for h4 in range(4):
                h = 4 * g4 + h4
                kslab = ba[:, gpar * 8192 + h4 * 1024: gpar * 8192 + (h4 + 1) * 1024]
                vslab = ba[:, gpar * 8192 + 4096 + h4 * 1024:
                           gpar * 8192 + 4096 + (h4 + 1) * 1024]

                sq_, parq = l1_ld(4 * g4 + h4)

                def mmq(te, sq_=sq_, parq=parq):
                    S.wait(te, "tensor", l1_cast[sq_], prev_silq[0])
                    i = None
                    for k in range(16):
                        i = mmW(te, PB, wbf[parq][:, 128 * k:128 * (k + 1)],
                                hch(hbf, k), (k == 0), (k == 15))
                    w_free[sq_] = S.inc(i, "T")
                secT(mmq)
                wguard[parq] = w_free[sq_]
                tok_mmq = S.now("T")

                def silq(sc, tok_mmq=tok_mmq):
                    S.wait(sc, "scalar", tok_mmq)
                    i = sc.activation(qT_h[:, :], PB[:, :], ACT.Silu)
                    prev_silq[0] = S.inc(i, "S")
                secS(silq)
                tok_q = S.now("S")

                def qdbs(sc, h=h):
                    S.wait(sc, "scalar", tok_tabs, tok_iot, prev_qdv[0])
                    S.inc(sc.activation(qdb[:, :], Iar[:, :], ACT.Exp,
                                        scale=snegc[:, h:h + 1]), "S")
                secS(qdbs)
                tok_qde = S.now("S")

                def qdv(v, tok_qde=tok_qde, tok_q=tok_q):
                    S.wait(v, "vector", tok_qde, tok_q)
                    i = None
                    for b in range(4):
                        i = v.tensor_mul(qd_h[:, 256 * b:256 * (b + 1)],
                                         qT_h[:, 256 * b:256 * (b + 1)], qdb[:, :])
                        S.inc(i, "V")
                    prev_qdv[0] = S.now("V")
                secV(qdv)
                tok_qd = S.now("V")

                def dds(sc, h=h):
                    S.wait(sc, "scalar", tok_tabs, tok_consts, tok_iot)
                    if prev_ddm[0] is not None:
                        S.wait(sc, "scalar", prev_ddm[0])
                    i = sc.activation(dd[:, :], I2[:, :], ACT.Exp,
                                      scale=snegc[:, h:h + 1])
                    S.inc(i, "S")
                secS(dds)
                tok_dde = S.now("S")

                def ddm(g, tok_dde=tok_dde):
                    S.wait(g, "gpsimd", tok_dde)
                    i = None
                    for kt in range(2):
                        i = g.affine_select(out=dd[:, 256 * kt:256 * (kt + 1)],
                                            in_=dd[:, 256 * kt:256 * (kt + 1)],
                                            compare_op=ALU.is_ge, fill=0.0,
                                            base=-128 * kt, pattern=[[1, 256]],
                                            channel_multiplier=-1)
                    S.inc(i, "G")
                secG(ddm)
                tok_dd = S.now("G")

                for c in range(2):
                    def scp(v, c=c, h4=h4):
                        S.wait(v, "vector", tok_sst, prev_oev[0])
                        i = v.tensor_copy(sbf[:, :],
                                          sst[:, (c * 4 + h4) * 128:
                                              (c * 4 + h4) * 128 + 128])
                        S.inc(i, "V")
                    secV(scp)
                    tok_state = S.now("V")

                    for bb in range(2):
                        b = 2 * c + bb
                        if bb == 1:
                            def smidv(v, c=c, h4=h4, h=h, gpar=gpar):
                                S.wait(v, "vector", tok_sst, prev_oev[0])
                                i = v.scalar_tensor_tensor(
                                    smid[:, :],
                                    sst[:, (c * 4 + h4) * 128:(c * 4 + h4) * 128 + 128],
                                    l256c[:, h:h + 1],
                                    ub0[:, gpar * 1024 + (c * 4 + h4) * 128:
                                        gpar * 1024 + (c * 4 + h4) * 128 + 128],
                                    ALU.mult, ALU.add)
                                S.inc(i, "V")
                                S.wait(v, "vector", S.now("V"))
                                S.inc(v.tensor_copy(sbf[:, :], smid[:, :]), "V")
                            secV(smidv)
                            tok_state = S.now("V")

                        def qkmm(te, b=b, kslab=kslab, tok_state=tok_state,
                                 tok_qd=tok_qd, tok_q=tok_q):
                            S.wait(te, "tensor", tok_state, tok_qd, tok_q,
                                   prev_oev[0])
                            i = te.matmul(PC[:, 256 * b:256 * (b + 1)], sbf[:, :],
                                          qd_h[:, 256 * b:256 * (b + 1)],
                                          start=True, stop=False)
                            S.inc(i, "T")
                            for kt in range(2):
                                i = te.matmul(PB[:, 512 * kt:512 * kt + 256],
                                              kslab[:, 256 * b + 128 * kt:
                                                     256 * b + 128 * (kt + 1)],
                                              qT_h[:, 256 * b:256 * (b + 1)],
                                              start=True, stop=True)
                                S.inc(i, "T")
                        secT(qkmm)
                        tok_qk = S.now("T")

                        def ddmul(v, tok_qk=tok_qk, tok_dd=tok_dd):
                            S.wait(v, "vector", tok_qk, tok_dd)
                            i = None
                            for kt in range(2):
                                i = v.tensor_mul(pbf[kt][:, 0:256],
                                                 PB[:, 512 * kt:512 * kt + 256],
                                                 dd[:, 256 * kt:256 * (kt + 1)])
                                S.inc(i, "V")
                            prev_ddm[0] = S.now("V")
                        secV(ddmul)
                        tok_qkdd = S.now("V")

                        def intra(te, b=b, vslab=vslab, tok_qkdd=tok_qkdd):
                            S.wait(te, "tensor", tok_qkdd)
                            i = None
                            for kt in range(2):
                                i = te.matmul(PC[:, 256 * b:256 * (b + 1)],
                                              vslab[:, 256 * b + 128 * kt:
                                                     256 * b + 128 * (kt + 1)],
                                              pbf[kt][:, 0:256],
                                              start=False, stop=(kt == 1))
                                S.inc(i, "T")
                        secT(intra)
                        tok_blk = S.now("T")

                        def oev(v, b=b, h=h, tok_blk=tok_blk):
                            S.wait(v, "vector", tok_blk)
                            i = v.tensor_copy(bc[:, 1024 * h + 256 * b:
                                                 1024 * h + 256 * (b + 1)],
                                              PC[:, 256 * b:256 * (b + 1)])
                            prev_oev[0] = S.inc(i, "V")
                        secV(oev)
            passB_done[g4] = S.now("V")

        emit_passA(0)
        if trunc == 42:
            cut_dram(agu_done[0], tok_spill)
            es_l1.close(); es_bc.close()
            return nc
        emit_passA(1)
        emit_passB(0)
        if trunc == 43:
            cut_dram(passB_done[0], tok_spill)
            es_l1.close(); es_bc.close()
            return nc
        emit_passA(2)
        emit_passB(1)
        emit_passA(3)
        emit_passB(2)
        emit_passB(3)
        tok_lattn = S.now("V")
        if trunc == 44:
            cut_dram(tok_lattn, tok_spill)
            es_l1.close(); es_bc.close()
            return nc

        sqt = {}
        for tq in range(2):
            t0 = tq * 512
            for c in range(CH):
                def sqr(sc, c=c, t0=t0):
                    S.wait(sc, "scalar", tok_lattn)
                    i = sc.activation(sq_r[c % 2][:, :],
                                      bc[:, c * T + t0:c * T + t0 + 512], ACT.Square)
                    sqt[c] = S.inc(i, "S")
                secS(sqr)

                def mms(te, c=c):
                    S.wait(te, "tensor", sqt[c])
                    i = te.matmul(PD[0:1, :], onescb[:, :], sq_r[c % 2][:, :],
                                  start=(c == 0), stop=(c == CH - 1))
                    sqt[c + 100] = S.inc(i, "T")
                secT(mms)
                if c < CH - 1:
                    def rgs(sc, c=c):
                        S.wait(sc, "scalar", sqt[c + 100])
                    secS(rgs)
            tok_ss = S.now("T")

            def rstd2v(v, tq=tq, tok_ss=tok_ss):
                S.wait(v, "vector", tok_ss)
                S.inc(v.tensor_scalar(rows2[0:1, 512 * tq:512 * (tq + 1)],
                                      PD[0:1, :], 1.0 / D, RMS_EPS,
                                      ALU.mult, ALU.add), "V")
            secV(rstd2v)
            tok_r2v = S.now("V")

            def rstd2(sc, tq=tq, tok_r2v=tok_r2v):
                S.wait(sc, "scalar", tok_r2v)
                i = sc.activation(rows2[0:1, 512 * tq:512 * (tq + 1)],
                                  rows2[0:1, 512 * tq:512 * (tq + 1)], ACT.Sqrt)
                S.inc(i, "S")
            secS(rstd2)
            tok_r2s = S.now("S")

            def rstd2r(v, tq=tq, tok_r2s=tok_r2s):
                S.wait(v, "vector", tok_r2s)
                S.inc(v.reciprocal(rows2[0:1, 512 * tq:512 * (tq + 1)],
                                   rows2[0:1, 512 * tq:512 * (tq + 1)]), "V")
            secV(rstd2r)
            tok_r2 = S.now("V")
            if tq == 0:
                def rg2(te, tok_r2=tok_r2):
                    S.wait(te, "tensor", tok_r2)
                secT(rg2)
        tok_rms = S.now("V")

        def rms_bc(te):
            S.wait(te, "tensor", tok_rms)
            S.inc(mmW(te, PB, onesr[:, :], rows2[0:1, :], True, True), "T")
        secT(rms_bc)
        tok_rmsb = S.now("T")

        gate_prev = {}

        def gate_consume(j, wb, ctok, rtok):
            pq = PA if j % 2 == 0 else PC

            def mmf(te, j=j, wb=wb, ctok=ctok, pq=pq):
                S.wait(te, "tensor", ctok, rtok, gate_prev.get(j - 2))
                i = None
                for k in range(16):
                    i = mmW(te, pq, wb[:, 128 * k:128 * (k + 1)],
                            hch(hbf, k), (k == 0), (k == 15))
                S.inc(i, "T")
            secT(mmf)
            tok_mm = S.now("T")

            def sig(sc, j=j, pq=pq, tok_mm=tok_mm):
                S.wait(sc, "scalar", tok_mm)
                i = sc.activation(gt_r[j % 2][:, :], pq[:, :], ACT.Sigmoid)
                gate_prev[j] = S.inc(i, "S")
            secS(sig)
            tok_g = S.now("S")

            def hmv(v, j=j, tok_g=tok_g):
                S.wait(v, "vector", tok_g, tok_rmsb, tok_sml)
                S.wait(v, "vector", S.now("V"))
                i = v.scalar_tensor_tensor(tmpf[:, :], hch(bc, j), nw_pp[:, j:j + 1],
                                           PB[:, :], ALU.mult, ALU.mult)
                S.inc(i, "V")
                S.wait(v, "vector", S.now("V"))
                i = v.tensor_mul(hch(ba, j), tmpf[:, :], gt_r[j % 2][:, :])
                gate_prev[j] = S.inc(i, "V")
            secV(hmv)
            return tok_mm

        stream_w(wgate1, 16, 16, gate_consume, tok_lattn)
        tok_hm1 = S.now("V")

        es_l1.close()
        es_bc.close()
        x_res = sb("x_res2", [128, CH * T], F32)
        tmpf2 = sb("tmpf2", [128, 1024], F32)
        m2t = sb("m2t2", [8, 1024], F32)
        esel = sb("esel", [8, 1024], F32)

        def ph7_sel_g(g):
            S.wait(g, "gpsimd", tok_hm1)
            S.inc(g.iota(esel[:, :], pattern=[[1, 8], [0, 128]], base=0,
                         channel_multiplier=-1,
                         allow_small_or_imprecise_dtypes=True), "G")
        secG(ph7_sel_g)
        tok_esel_g = S.now("G")

        def ph7_sel_v(v):
            S.wait(v, "vector", tok_esel_g)
            S.inc(v.tensor_scalar(esel[:, :], esel[:, :], 0.0, None,
                                  ALU.is_equal), "V")
        secV(ph7_sel_v)
        tok_esel = S.now("V")

        def reload_g(g):
            S.wait(g, "gpsimd", tok_hm1)
            for c in range(CH):
                S.inc(g.dma_start(out=hch(x_res, c), in_=xsp[c:c + 1]), "qG", 16)
        secG(reload_g)
        tok_rl_all = S.now("qG")
        rl_tok = {c: tok_rl_all for c in range(CH)}

        wout_prev = {}

        def wout_consume(j, wb, ctok, rtok):
            pq = PA if j % 2 == 0 else PC

            def mmf(te, j=j, wb=wb, ctok=ctok, pq=pq):
                S.wait(te, "tensor", ctok, rtok, wout_prev.get(j - 2))
                i = None
                for k in range(16):
                    i = mmW(te, pq, wb[:, 128 * k:128 * (k + 1)],
                            hch(ba, k), (k == 0), (k == 15))
                S.inc(i, "T")
            secT(mmf)
            tok_mm = S.now("T")

            def evf(v, j=j, pq=pq, tok_mm=tok_mm):
                S.wait(v, "vector", tok_mm, rl_tok[j])
                i = v.scalar_tensor_tensor(hch(x_res, j), pq[:, :],
                                           mod_pp[:, 96 + 32 + j:96 + 33 + j],
                                           hch(x_res, j), ALU.mult, ALU.add)
                wout_prev[j] = S.inc(i, "V")
            secV(evf)
            return tok_mm

        stream_w(wout1, 16, 16, wout_consume, tok_hm1)
        tok_x3 = S.now("V")

        if DEBUG_DUMPS:
            def dmp3(g, tok_x3=tok_x3):
                S.wait(g, "gpsimd", tok_x3)
                for c in range(CH):
                    S.inc(g.dma_start(out=xd3[c:c + 1], in_=hch(x_res, c)),
                          "qG", 16)
            secG(dmp3)
            tok_x3 = S.now("qG")

        if trunc <= 6:
            cut_out(tok_x3)
            return nc
        tok_ln4 = emit_ln(1, True, extra=(tok_x3,))

        def rt_consume(j, wb, ctok, rtok):
            def mmf(te, wb=wb, ctok=ctok):
                S.wait(te, "tensor", ctok, rtok)
                i = None
                for k in range(16):
                    i = mmW(te, PA[0:8, :], wb[:, 8 * k:8 * (k + 1)],
                            hch(hbf, k), (k == 0), (k == 15))
                S.inc(i, "T")
            secT(mmf)
            return S.now("T")

        stream_w(wrt, 1, 16, rt_consume, tok_ln4, mw=8)
        tok_rl = S.now("T")

        def rl_cp(v):
            S.wait(v, "vector", tok_rl)
            S.inc(v.tensor_copy(m2t[:, :], PA[0:8, :]), "V")
        secV(rl_cp)
        tok_rlc = S.now("V")

        def rl_tr(te):
            S.wait(te, "tensor", tok_rlc)
            i = None
            for tt in range(8):
                i = te.transpose(PD[:, 8 * tt:8 * tt + 8],
                                 m2t[0:8, 128 * tt:128 * (tt + 1)], identf[0:8, 0:8])
            S.inc(i, "T")
        secT(rl_tr)
        tok_rtr = S.now("T")

        def rl_tok(v):
            S.wait(v, "vector", tok_rtr)
            S.inc(v.tensor_copy(ltok[:, :], PD[:, 0:64]), "V")
        secV(rl_tok)
        tok_rltok = S.now("V")

        gtr_toks = {}
        for tt in range(8):
            def top2(v, tt=tt):
                S.wait(v, "vector", tok_rltok, gtr_toks.get(tt - 1))
                lt = ltok[:, 8 * tt:8 * (tt + 1)]
                S.inc(v.reduce_max(scol[:, 0:1], lt, axis=AX, negate=True), "V")
                S.wait(v, "vector", S.now("V"))
                S.inc(v.tensor_scalar(scol[:, 2:10], lt, scol[:, 0:1], 0.0,
                                      ALU.add, ALU.is_equal), "V")
                S.wait(v, "vector", S.now("V"))
                S.inc(v.scalar_tensor_tensor(scol[:, 10:18], scol[:, 2:10], -1e30,
                                             lt, ALU.mult, ALU.add), "V")
                S.wait(v, "vector", S.now("V"))
                S.inc(v.reduce_max(scol[:, 1:2], scol[:, 10:18], axis=AX,
                                   negate=True), "V")
                S.wait(v, "vector", S.now("V"))
                S.inc(v.tensor_scalar(scol[:, 10:18], scol[:, 10:18], scol[:, 1:2],
                                      0.0, ALU.add, ALU.is_equal), "V")
                S.inc(v.tensor_sub(scol[:, 20:21], scol[:, 0:1], scol[:, 1:2]), "V")
            secV(top2)
            tok_t2 = S.now("V")

            def t2e(sc, tok_t2=tok_t2):
                S.wait(sc, "scalar", tok_t2)
                S.inc(sc.activation(scol[:, 21:22], scol[:, 20:21], ACT.Exp), "S")
            secS(t2e)
            tok_t2e = S.now("S")

            def t2w(v, tt=tt, tok_t2e=tok_t2e):
                S.wait(v, "vector", tok_t2e)
                S.inc(v.tensor_scalar(scol[:, 21:22], scol[:, 21:22], 1.0, None,
                                      ALU.add), "V")
                S.wait(v, "vector", S.now("V"))
                S.inc(v.reciprocal(scol[:, 22:23], scol[:, 21:22]), "V")
                S.wait(v, "vector", S.now("V"))
                S.inc(v.tensor_scalar(scol[:, 23:24], scol[:, 22:23], -1.0, 1.0,
                                      ALU.mult, ALU.add), "V")
                S.wait(v, "vector", S.now("V"))
                S.inc(v.tensor_scalar(scol[:, 2:10], scol[:, 2:10], scol[:, 22:23],
                                      None, ALU.mult), "V")
                S.wait(v, "vector", S.now("V"))
                S.inc(v.scalar_tensor_tensor(scol[:, 2:10], scol[:, 10:18],
                                             scol[:, 23:24], scol[:, 2:10],
                                             ALU.mult, ALU.add), "V")
            secV(t2w)
            tok_g2 = S.now("V")

            def gtr(te, tt=tt, tok_g2=tok_g2):
                S.wait(te, "tensor", tok_g2)
                i = te.transpose(PD[0:8, 128 * (tt % 4):128 * (tt % 4) + 128],
                                 scol[:, 2:10], identf[:, :])
                gtr_toks[tt] = S.inc(i, "T")
            secT(gtr)

            if tt % 4 == 3:
                def gcp(v, tt=tt):
                    S.wait(v, "vector", gtr_toks[tt])
                    i = v.tensor_copy(m2t[0:8, 512 * (tt // 4):512 * (tt // 4 + 1)],
                                      PD[0:8, :])
                    S.inc(i, "V")
                secV(gcp)
        tok_gT = S.now("V")

        moe_cast = {}
        moe_load = {}
        moe_free = {}
        mslot = [0]

        def moe_ld(dr, e, m, kc):
            s = mslot[0]
            mslot[0] += 1
            par = s % 2

            def ld(q, s=s, par=par, dr=dr, e=e, m=m, kc=kc, pc=wchain["c"]):
                S.wait(q, "sync", pc)
                i = q.dma_start(
                    out=wst[par][:, 0:kc * 128].rearrange("p (k m) -> p k m", k=kc),
                    in_=dr[e:e + 1, m:m + 1])
                S.inc(i, "qS", 16)
                moe_load[s] = S.now("qS")
            secQ(ld)

            def cst(g, s=s, par=par, kc=kc, pg=wguard[par]):
                S.wait(g, "gpsimd", moe_load[s], pg)
                i = g.tensor_copy(wbf[par][:, 0:kc * 128], wst[par][:, 0:kc * 128])
                moe_cast[s] = S.inc(i, "G")
            secG(cst)
            wchain["c"] = moe_cast[s]
            return s, par

        prev_zv = [None]
        prev_sil = [None]
        prev_yacc = [None]

        for e in range(NE):
            def gbc(te, e=e):
                S.wait(te, "tensor", tok_gT, tok_esel, prev_zv[0])
                S.inc(mmW(te, PB, esel[:, 128 * e:128 * (e + 1)],
                          m2t[0:8, :], True, True), "T")
            secT(gbc)
            tok_gbc = S.now("T")

            for fm in range(8):
                s1, p1 = moe_ld(w1m, e, fm, 16)
                s2, p2 = moe_ld(w2m, e, fm, 16)

                def mm12(te, s1=s1, s2=s2, p1=p1, p2=p2):
                    S.wait(te, "tensor", moe_cast[s2], tok_ln4, prev_sil[0],
                           prev_zv[0], prev_yacc[0])
                    i = None
                    for k in range(16):
                        i = mmW(te, PA, wbf[p1][:, 128 * k:128 * (k + 1)],
                                hch(hbf, k), (k == 0), (k == 15))
                    moe_free[s1] = S.inc(i, "T")
                    for k in range(16):
                        i = mmW(te, PC, wbf[p2][:, 128 * k:128 * (k + 1)],
                                hch(hbf, k), (k == 0), (k == 15))
                    moe_free[s2] = S.inc(i, "T")
                secT(mm12)
                wguard[p1] = moe_free[s1]
                wguard[p2] = moe_free[s2]
                tok_uv = S.now("T")

                def sil(sc, tok_uv=tok_uv):
                    S.wait(sc, "scalar", tok_uv)
                    i = sc.activation(tmpf[:, :], PA[:, :], ACT.Silu)
                    prev_sil[0] = S.inc(i, "S")
                secS(sil)
                tok_sil = S.now("S")

                def zv(v, fm=fm, tok_sil=tok_sil, tok_gbc=tok_gbc):
                    S.wait(v, "vector", tok_sil, tok_gbc)
                    S.wait(v, "vector", S.now("V"))
                    S.inc(v.tensor_mul(tmpf2[:, :], tmpf[:, :], PC[:, :]), "V")
                    S.wait(v, "vector", S.now("V"))
                    i = v.tensor_mul(hch(ba, fm), tmpf2[:, :], PB[:, :])
                    prev_zv[0] = S.inc(i, "V")
                secV(zv)
            tok_z = S.now("V")

            for om in range(16):
                s3, p3 = moe_ld(w3m, e, om, 8)
                pq = PA if om % 2 == 0 else PC

                def mm3(te, s3=s3, p3=p3, pq=pq, tok_z=tok_z, om=om):
                    S.wait(te, "tensor", moe_cast[s3], tok_z)
                    if om >= 2:
                        S.wait(te, "tensor", prev_yacc[0])
                    i = None
                    for k in range(8):
                        i = mmW(te, pq, wbf[p3][:, 128 * k:128 * (k + 1)],
                                hch(ba, k), (k == 0), (k == 7))
                    moe_free[s3] = S.inc(i, "T")
                secT(mm3)
                wguard[p3] = moe_free[s3]
                tok_y = S.now("T")

                def yacc(v, om=om, pq=pq, tok_y=tok_y):
                    S.wait(v, "vector", tok_y)
                    i = v.scalar_tensor_tensor(hch(x_res, om), pq[:, :],
                                               mod_pp[:, 96 + 80 + om:96 + 81 + om],
                                               hch(x_res, om), ALU.mult, ALU.add)
                    prev_yacc[0] = S.inc(i, "V")
                secV(yacc)
        tok_xf = S.now("V")

        def outst(q):
            S.wait(q, "sync", tok_xf)
            for c in range(CH):
                S.inc(q.dma_start(out=outT[c:c + 1], in_=hch(x_res, c)), "qS", 16)
            S.wait(q, "sync", S.now("qS"))
        secQ(outst)

    return nc



def _mc(w):
    w = np.ascontiguousarray(w, dtype=np.float32)
    K, M = w.shape
    return np.ascontiguousarray(
        w.reshape(K // 128, 128, M // 128, 128).transpose(2, 1, 0, 3))


def _mc_narrow(w, mwidth):
    w = np.ascontiguousarray(w, dtype=np.float32)
    K, M = w.shape
    return np.ascontiguousarray(
        w.reshape(K // 128, 128, 1, M).transpose(2, 1, 0, 3))


TRUNC = 99


def kernel(**inputs):
    x = np.asarray(inputs["L0_attn_x"], np.float32)
    wqkv0 = _mc(np.asarray(inputs["L0_attn_w_qkv"], np.float32))
    b_qkv = np.asarray(inputs["L0_attn_b_qkv"], np.float32)
    bq = np.ascontiguousarray(b_qkv[:2048].reshape(16, 128).T)
    bk = np.ascontiguousarray(b_qkv[2048:2176].reshape(128, 1))
    bv = np.ascontiguousarray(b_qkv[2176:2304].reshape(1, 128))
    wfc = _mc(np.asarray(inputs["L0_attn_w_fc"], np.float32))
    bfc = np.ascontiguousarray(np.asarray(inputs["L0_attn_b_fc"], np.float32)
                               .reshape(16, 128).T)
    A = np.asarray(inputs["L0_ffn_A"], np.float32)
    a_t = _mc(np.ascontiguousarray(A.reshape(512, 2048).T))
    Bw = np.asarray(inputs["L0_ffn_Bw"], np.float32)
    bw_f = _mc(np.ascontiguousarray(Bw.transpose(0, 2, 1)).reshape(512, 2048))
    wg_t = _mc_narrow(np.ascontiguousarray(
        np.asarray(inputs["L0_ffn_Wg"], np.float32).T), 8)
    c0 = np.asarray(inputs["L0_c"], np.float32)
    wmod0 = np.asarray(inputs["L0_wmod"], np.float32)
    slope = np.asarray(inputs["L1_attn_slope_rate"], np.float32).reshape(1, 16)
    wqkv1 = _mc(np.ascontiguousarray(np.asarray(inputs["L1_attn_w_qkv"],
                                                np.float32).T))
    wgate1 = _mc(np.ascontiguousarray(np.asarray(inputs["L1_attn_w_gate"],
                                                 np.float32).T))
    wout1 = _mc(np.ascontiguousarray(np.asarray(inputs["L1_attn_w_out"],
                                                np.float32).T))
    nw1 = np.ascontiguousarray(np.asarray(inputs["L1_attn_norm_w"], np.float32)
                               .reshape(16, 128).T)
    wrt = _mc_narrow(np.ascontiguousarray(
        np.asarray(inputs["L1_ffn_router_w"], np.float32).T), 8)
    w1m = np.stack([_mc(np.asarray(inputs["L1_ffn_w1"], np.float32)[e])
                    for e in range(8)])
    w2m = np.stack([_mc(np.asarray(inputs["L1_ffn_w2"], np.float32)[e])
                    for e in range(8)])
    w3m = np.stack([_mc(np.asarray(inputs["L1_ffn_w3"], np.float32)[e])
                    for e in range(8)])
    c1 = np.asarray(inputs["L1_c"], np.float32)
    wmod1 = np.asarray(inputs["L1_wmod"], np.float32)

    nc = build_nc(TRUNC)
    in_maps = []
    for i in range(NCORE):
        xs = np.ascontiguousarray(
            x[0, 1024 * i:1024 * (i + 1), :].T.reshape(16, 128, 1024))
        c01 = np.stack([c0[0, 256 * i:256 * (i + 1)].reshape(2, 128),
                        c1[0, 256 * i:256 * (i + 1)].reshape(2, 128)])
        wmod_i = np.stack([wmod0[256 * i:256 * (i + 1), :].reshape(2, 128, 12288),
                           wmod1[256 * i:256 * (i + 1), :].reshape(2, 128, 12288)])
        ev = np.zeros((1, 32), np.float32)
        for cc in range(2):
            cg = 2 * i + cc
            for j in range(16):
                ev[0, 16 * cc + j] = (cg - 1 - j) if j < cg else 30000.0
        co = np.zeros((1, 128), np.float32)
        for qt in range(2):
            for t in range(64):
                co[0, 64 * qt + t] = 1024 * i + 512 * qt - 128 * t
        in_maps.append({
            "xT": xs, "wqkv0": wqkv0, "bq": bq, "bk": bk, "bv": bv,
            "wfc": wfc, "bfc": bfc, "a_t": a_t, "bw_f": bw_f, "wg_t": wg_t,
            "c01": c01, "wmod": wmod_i, "slope": slope, "wqkv1": wqkv1,
            "wgate1": wgate1, "wout1": wout1, "nw1": nw1, "wrt": wrt,
            "w1m": w1m, "w2m": w2m, "w3m": w3m, "evec": ev, "coff": co,
        })

    res = run_bass_kernel_spmd(nc, in_maps, core_ids=list(range(NCORE)))
    out = np.concatenate(
        [np.asarray(r["outT"]).reshape(2048, 1024).T for r in res.results], axis=0)
    return np.ascontiguousarray(out.reshape(1, 8192, 2048).astype(np.float32))
